# version 04
# baseline (speedup 1.0000x reference)
import contextlib

import numpy as np
import ml_dtypes

import concourse.bass as bass
import concourse.mybir as mybir
from concourse.bass_utils import run_bass_kernel_spmd

F32 = mybir.dt.float32
BF16 = mybir.dt.bfloat16
AF = mybir.ActivationFunctionType
ALU = mybir.AluOpType
AX = mybir.AxisListType
bfloat16 = ml_dtypes.bfloat16

D_MODEL = 1024
NCH = 8
T_CORE = 2048
N_CORES = 8
EPS = 1e-5
FF = 4096
NF = FF // 128


def host_consts():
    c = {}
    c["c_idf"] = np.eye(128, dtype=np.float32)
    c["c_ones_row"] = np.ones((1, 128), dtype=np.float32)
    oh = np.zeros((16, 16 * 128), dtype=np.float32)
    for e in range(16):
        oh[e, 128 * e:128 * e + 128] = 1.0
    c["c_oh"] = oh
    c["c_rank16"] = np.broadcast_to(
        (16.0 - np.arange(16, dtype=np.float32)), (128, 16)).copy()
    c["c_rank8"] = np.broadcast_to(
        (8.0 - np.arange(8, dtype=np.float32)), (128, 8)).copy()
    c["c_ones_col"] = np.ones((128, 1), dtype=np.float32)
    c["c_idb"] = np.eye(128, dtype=np.float32).astype(bfloat16)
    hm = np.zeros((128, 2), dtype=np.float32)
    hm[0:64, 0] = 1.0
    hm[64:128, 1] = 1.0
    c["c_hmask_f"] = hm
    c["c_hmask_b"] = hm.astype(bfloat16)
    c["c_sel2"] = np.ascontiguousarray(hm.T)
    P1 = np.zeros((64, 64), dtype=np.float32)
    for d in range(32):
        P1[d, 2 * d + 1] = -1.0
        P1[32 + d, 2 * d] = 1.0
    P2 = np.zeros((128, 128), dtype=np.float32)
    P2[0:64, 0:64] = P1
    P2[64:128, 64:128] = P1
    c["c_prot2"] = np.ascontiguousarray(P2.T).astype(bfloat16)
    return c


CONST_NAMES = list(host_consts().keys())


class Sem:
    def __init__(self, h):
        self.h = h
        self.n = 0


class Ctx:
    def __init__(self, nc, block, st):
        self.nc = nc
        self.blk = block
        self.st = st
        self.sem = {}
        self.names = {}

    def S(self, name):
        if name not in self.sem:
            self.sem[name] = Sem(self.st.enter_context(self.nc.semaphore(name)))
        return self.sem[name]

    def _nm(self, name):
        k = self.names.get(name, 0)
        self.names[name] = k + 1
        return name if k == 0 else f"{name}_{k}"

    def sbuf(self, name, shape, dt, scope=None):
        return (scope or self.st).enter_context(
            self.nc.sbuf_tensor(self._nm(name), shape, dt))

    def psum(self, name, shape, dt, scope=None):
        return (scope or self.st).enter_context(
            self.nc.psum_tensor(self._nm(name), shape, dt))


def emit_barrier(E):
    engines = ("tensor", "vector", "scalar", "sync", "gpsimd")
    targets = [(s.h, s.n) for s in E.sem.values()]
    for e in engines:
        def mk():
            def f(q):
                for h, n in targets:
                    q.wait_ge(h, n)
            return f
        getattr(E.blk, e)(mk())


def emit_load_x(E, scope, x_dram, x_res, n_tok, idf):
    ntile = n_tok // 128
    xin = E.sbuf("ld_xin", [128, 2 * D_MODEL], F32, scope)
    pt = E.psum("ld_pt", [128, 1024], F32, scope)
    d = [E.S("d_ldx0"), E.S("d_ldx1")]
    pe = E.S("pe_ldx")
    ve = E.S("ve_ldx")
    db, peb, veb = [s.n for s in d], pe.n, ve.n

    def off(i):
        return ((i // 4) % 2) * 512 + (i % 4) * 128

    @E.blk.gpsimd
    def _(g):
        for t in range(ntile):
            if t >= 2:
                g.wait_ge(pe.h, peb + 8 * (t - 1))
            g.dma_start(out=xin[:, (t % 2) * D_MODEL:(t % 2 + 1) * D_MODEL],
                        in_=x_dram[128 * t:128 * (t + 1), :]).then_inc(d[t % 2].h, 16)
    for t in range(ntile):
        d[t % 2].n += 16

    @E.blk.tensor
    def _(t_):
        for t in range(ntile):
            t_.wait_ge(d[t % 2].h, db[t % 2] + 16 * (t // 2 + 1))
            for c in range(NCH):
                i = t * 8 + c
                if i % 4 == 0 and i >= 8:
                    t_.wait_ge(ve.h, veb + i - 4)
                t_.transpose(pt[:, off(i):off(i) + 128],
                             xin[:, (t % 2) * D_MODEL + 128 * c:
                                 (t % 2) * D_MODEL + 128 * (c + 1)],
                             idf[:, :]).then_inc(pe.h, 1)
    pe.n += 8 * ntile

    @E.blk.vector
    def _(v):
        for t in range(ntile):
            for c in range(NCH):
                i = t * 8 + c
                if i % 4 == 0:
                    v.wait_ge(pe.h, peb + i + 4)
                v.tensor_copy(x_res[:, n_tok * c + 128 * t:n_tok * c + 128 * (t + 1)],
                              pt[:, off(i):off(i) + 128]).then_inc(ve.h, 1)
    ve.n += 8 * ntile


def emit_store_x(E, scope, x_res, y_dram, n_tok, idf):
    ntile = n_tok // 128
    xout = E.sbuf("st_xout", [128, 2 * D_MODEL], F32, scope)
    pt = E.psum("st_pt", [128, 1024], F32, scope)
    d = [E.S("d_stx0"), E.S("d_stx1")]
    pe = E.S("pe_stx")
    ve = E.S("ve_stx")
    db, peb, veb = [s.n for s in d], pe.n, ve.n

    def off(i):
        return ((i // 4) % 2) * 512 + (i % 4) * 128

    @E.blk.tensor
    def _(t_):
        for t in range(ntile):
            for c in range(NCH):
                i = t * 8 + c
                if i % 4 == 0 and i >= 8:
                    t_.wait_ge(ve.h, veb + i - 4)
                t_.transpose(pt[:, off(i):off(i) + 128],
                             x_res[:, n_tok * c + 128 * t:n_tok * c + 128 * (t + 1)],
                             idf[:, :]).then_inc(pe.h, 1)
    pe.n += 8 * ntile

    @E.blk.vector
    def _(v):
        for t in range(ntile):
            if t >= 2:
                v.wait_ge(d[t % 2].h, db[t % 2] + 16 * ((t - 2) // 2 + 1))
            for c in range(NCH):
                i = t * 8 + c
                if i % 4 == 0:
                    v.wait_ge(pe.h, peb + i + 4)
                v.tensor_copy(xout[:, (t % 2) * D_MODEL + 128 * c:
                                   (t % 2) * D_MODEL + 128 * (c + 1)],
                              pt[:, off(i):off(i) + 128]).then_inc(ve.h, 1)
    ve.n += 8 * ntile

    @E.blk.sync
    def _(q):
        for t in range(ntile):
            q.wait_ge(ve.h, veb + 8 * (t + 1))
            q.dma_start(out=y_dram[128 * t:128 * (t + 1), :],
                        in_=xout[:, (t % 2) * D_MODEL:(t % 2 + 1) * D_MODEL]
                        ).then_inc(d[t % 2].h, 16)
        q.wait_ge(d[0].h, db[0] + 16 * ((ntile + 1) // 2))
        q.wait_ge(d[1].h, db[1] + 16 * (ntile // 2))
    for t in range(ntile):
        d[t % 2].n += 16


def emit_ln(E, scope, x_res, xn, n_tok, ones_row, ones_col, xn_stride=None,
            xn_off=0):
    if xn_stride is None:
        xn_stride = n_tok
    sq = E.sbuf("ln_sq", [128, n_tok], F32, scope)
    mrow = E.sbuf("ln_m", [1, n_tok], F32, scope)
    rrow = E.sbuf("ln_r", [1, n_tok], F32, scope)
    m2row = E.sbuf("ln_m2", [1, n_tok], F32, scope)
    mb = E.sbuf("ln_mb", [128, n_tok], F32, scope)
    rb = E.sbuf("ln_rb", [128, n_tok], F32, scope)
    pr = E.psum("ln_pr", [1, n_tok], F32, scope)
    pb = E.psum("ln_pb", [128, 1024], F32, scope)
    pe = E.S("pe_ln")
    ve = E.S("ve_ln")
    ac = E.S("ac_ln")
    peb, veb, acb = pe.n, ve.n, ac.n
    nj = n_tok // 512

    @E.blk.tensor
    def _(t_):
        for c in range(NCH):
            for j in range(nj):
                mm = t_.matmul(pr[0:1, 512 * j:512 * (j + 1)], ones_col[:, 0:1],
                               x_res[:, n_tok * c + 512 * j:n_tok * c + 512 * (j + 1)],
                               start=(c == 0), stop=(c == NCH - 1))
                if c == NCH - 1:
                    mm.then_inc(pe.h, 1)
    pe.n += nj

    @E.blk.vector
    def _(v):
        v.wait_ge(pe.h, peb + nj)
        v.tensor_scalar(mrow[0:1, :], pr[0:1, :], 1.0 / D_MODEL, None,
                        op0=ALU.mult).then_inc(ve.h, 1)
    ve.n += 1

    @E.blk.scalar
    def _(s):
        for c in range(NCH):
            if c > 0:
                s.wait_ge(pe.h, peb + nj + nj * c)
            s.activation(out=sq[:, :], in_=x_res[:, n_tok * c:n_tok * (c + 1)],
                         func=AF.Square).then_inc(ac.h, 1)
    ac.n += NCH

    @E.blk.tensor
    def _(t_):
        t_.wait_ge(ve.h, veb + 1)
        for c in range(NCH):
            t_.wait_ge(ac.h, acb + c + 1)
            for j in range(nj):
                t_.matmul(pr[0:1, 512 * j:512 * (j + 1)], ones_col[:, 0:1],
                          sq[:, 512 * j:512 * (j + 1)],
                          start=(c == 0), stop=(c == NCH - 1)).then_inc(pe.h, 1)
    pe.n += NCH * nj
    pe_ssq = pe.n

    @E.blk.vector
    def _(v):
        v.wait_ge(pe.h, pe_ssq)
        v.wait_ge(ve.h, veb + 1)
        v.tensor_scalar(rrow[0:1, :], pr[0:1, :], 1.0 / D_MODEL, None,
                        op0=ALU.mult).then_inc(ve.h, 1)
        v.tensor_tensor(m2row[0:1, :], mrow[0:1, :], mrow[0:1, :],
                        op=ALU.mult).then_inc(ve.h, 1)
        v.wait_ge(ve.h, veb + 3)
        v.tensor_tensor(rrow[0:1, :], rrow[0:1, :], m2row[0:1, :],
                        op=ALU.subtract).then_inc(ve.h, 1)
        v.wait_ge(ve.h, veb + 4)
        v.tensor_scalar(rrow[0:1, :], rrow[0:1, :], EPS, None,
                        op0=ALU.add).then_inc(ve.h, 1)
    ve.n += 4

    @E.blk.scalar
    def _(s):
        s.wait_ge(ve.h, veb + 5)
        s.activation(out=rrow[0:1, :], in_=rrow[0:1, :], func=AF.Sqrt).then_inc(ac.h, 1)
    ac.n += 1

    @E.blk.vector
    def _(v):
        v.wait_ge(ac.h, acb + NCH + 1)
        v.reciprocal(rrow[0:1, :], rrow[0:1, :]).then_inc(ve.h, 1)
    ve.n += 1
    ve_rdy = ve.n

    peb2 = pe.n
    @E.blk.tensor
    def _(t_):
        t_.wait_ge(ve.h, ve_rdy)
        for j in range(nj):
            if j >= 1:
                t_.wait_ge(ve.h, ve_rdy + 2 * j)
            t_.matmul(pb[:, 0:512], ones_row[0:1, :], mrow[0:1, 512 * j:512 * (j + 1)],
                      start=True, stop=True).then_inc(pe.h, 1)
            t_.matmul(pb[:, 512:1024], ones_row[0:1, :], rrow[0:1, 512 * j:512 * (j + 1)],
                      start=True, stop=True).then_inc(pe.h, 1)
    pe.n += 2 * nj

    @E.blk.vector
    def _(v):
        for j in range(nj):
            v.wait_ge(pe.h, peb2 + 2 * j + 2)
            v.tensor_copy(mb[:, 512 * j:512 * (j + 1)], pb[:, 0:512]).then_inc(ve.h, 1)
            v.tensor_copy(rb[:, 512 * j:512 * (j + 1)], pb[:, 512:1024]).then_inc(ve.h, 1)
        for c in range(NCH):
            v.wait_ge(ve.h, ve_rdy + 2 * nj + 2 * c)
            v.tensor_tensor(sq[:, :], x_res[:, n_tok * c:n_tok * (c + 1)], mb[:, :],
                            op=ALU.subtract).then_inc(ve.h, 1)
            v.wait_ge(ve.h, ve_rdy + 2 * nj + 2 * c + 1)
            v.tensor_tensor(xn[:, xn_stride * c + xn_off:
                               xn_stride * c + xn_off + n_tok],
                            sq[:, :], rb[:, :], op=ALU.mult).then_inc(ve.h, 1)
    ve.n += 2 * nj + 2 * NCH


def emit_convert_w(E, scope, src_f32, dst_bf16, rows, cols, tag):
    w = min(cols, 4096)
    buf = E.sbuf("cw_" + tag, [128, 2 * w], BF16, scope)
    din = [E.S("d_cwi0"), E.S("d_cwi1")]
    dout = [E.S("d_cwo0"), E.S("d_cwo1")]
    dib = [s.n for s in din]
    dob = [s.n for s in dout]
    nr = rows // 128
    ncs = cols // w

    @E.blk.gpsimd
    def _(g):
        i = 0
        for r in range(nr):
            for cc in range(ncs):
                sl = i % 2
                if i >= 2:
                    g.wait_ge(dout[sl].h, dob[sl] + 16 * (i // 2))
                g.dma_start(out=buf[:, sl * w:(sl + 1) * w],
                            in_=src_f32[128 * r:128 * (r + 1), w * cc:w * (cc + 1)]
                            ).then_inc(din[sl].h, 16)
                g.wait_ge(din[sl].h, dib[sl] + 16 * (i // 2 + 1))
                g.dma_start(out=dst_bf16[128 * r:128 * (r + 1), w * cc:w * (cc + 1)],
                            in_=buf[:, sl * w:(sl + 1) * w]).then_inc(dout[sl].h, 16)
                i += 1
    n = nr * ncs
    for i in range(n):
        din[i % 2].n += 16
        dout[i % 2].n += 16


def emit_unipool_ffn(E, scope, x_res, xn, n_tok, cst, routers_bf, w1_bf, w2_bf,
                     nlayers=4, nexperts=16, res_scale=0.5, tag="u", debug=None):
    E8 = nexperts
    TT = 512
    ngrp = n_tok // TT
    NS = 8
    NP2 = 8

    w1r = E.sbuf(tag + "_w1r", [128, 2 * NCH * 512], BF16, scope)
    w2r = E.sbuf(tag + "_w2r", [128, 2 * 4 * D_MODEL], BF16, scope)
    hp = E.sbuf(tag + "_hp", [128, NF * TT], BF16, scope)
    acc = E.sbuf(tag + "_acc", [128, NCH * TT], F32, scope)
    tmp = E.sbuf(tag + "_tmp", [128, TT], F32, scope)
    selb = E.sbuf(tag + "_selb", [128, 2 * TT], F32, scope)
    selT = E.sbuf(tag + "_selT", [16, TT], F32, scope)
    logi = E.sbuf(tag + "_logi", [128, (TT // 128) * E8], F32, scope)
    seltm = E.sbuf(tag + "_seltm", [128, (TT // 128) * E8], F32, scope)
    srow = E.sbuf(tag + "_srow", [128, 7 * (TT // 128)], F32, scope)
    P = E.psum(tag + "_P", [128, 4096], F32, scope)

    d1 = [E.S("d_w1a_" + tag), E.S("d_w1b_" + tag)]
    d2 = [E.S("d_w2a_" + tag), E.S("d_w2b_" + tag)]
    pe = E.S("pe_" + tag)
    ac_ = E.S("ac_" + tag)
    ve = E.S("ve_" + tag)

    prev_step_ve = ve.n
    ntt = TT // 128
    PE_E = 72
    VE_E = 17

    for g in range(ngrp):
        for l in range(nlayers):
            pe0, ve0, ac0 = pe.n, ve.n, ac_.n
            psve = prev_step_ve

            @E.blk.tensor
            def _(t_, g=g, l=l, pe0=pe0, ve0=ve0, psve=psve):
                t_.wait_ge(ve.h, psve)
                for tt in range(ntt):
                    gt = g * ntt + tt
                    bank = 6 + (tt % 2)
                    col = 512 * bank + 16 * (tt // 2)
                    if tt >= 2:
                        t_.wait_ge(ve.h, ve0 + 9 * (tt - 2) + 1)
                    for c in range(NCH):
                        mm = t_.matmul(
                            P[:, col:col + E8],
                            xn[:, n_tok * c + 128 * gt:n_tok * c + 128 * (gt + 1)],
                            routers_bf[:, (l * NCH + c) * 16:(l * NCH + c) * 16 + E8],
                            start=(c == 0), stop=(c == NCH - 1))
                        if c == NCH - 1:
                            mm.then_inc(pe.h, 1)
            pe.n += ntt

            @E.blk.vector
            def _(v, g=g, pe0=pe0, ve0=ve0, ac0=ac0):
                k = ve0
                for tt in range(ntt):
                    bank = 6 + (tt % 2)
                    col = 512 * bank + 16 * (tt // 2)
                    L = logi[:, tt * E8:(tt + 1) * E8]
                    SE = seltm[:, tt * E8:(tt + 1) * E8]
                    c0 = 7 * tt
                    v.wait_ge(pe.h, pe0 + tt + 1)
                    v.tensor_copy(L, P[:, col:col + E8]).then_inc(ve.h, 1); k += 1
                    v.wait_ge(ve.h, k)
                    v.tensor_reduce(srow[:, c0:c0 + 1], L, axis=AX.X, op=ALU.max,
                                    negate=True).then_inc(ve.h, 1); k += 1
                    v.wait_ge(ac_.h, ac0 + tt + 1)
                    v.reciprocal(srow[:, c0 + 2:c0 + 3],
                                 srow[:, c0 + 1:c0 + 2]).then_inc(ve.h, 1); k += 1
                    v.wait_ge(ve.h, k)
                    v.tensor_scalar(L, L, srow[:, c0 + 2:c0 + 3], None,
                                    op0=ALU.mult).then_inc(ve.h, 1); k += 1
                    v.wait_ge(ve.h, k)
                    v.tensor_reduce(srow[:, c0 + 3:c0 + 4], L, axis=AX.X,
                                    op=ALU.max).then_inc(ve.h, 1); k += 1
                    v.wait_ge(ve.h, k)
                    v.tensor_scalar(SE, L, srow[:, c0 + 3:c0 + 4], None,
                                    op0=ALU.is_equal).then_inc(ve.h, 1); k += 1
                    v.wait_ge(ve.h, k)
                    v.tensor_tensor(SE, SE, cst["rank"][:, 0:E8],
                                    op=ALU.mult).then_inc(ve.h, 1); k += 1
                    v.wait_ge(ve.h, k)
                    v.tensor_reduce(srow[:, c0 + 4:c0 + 5], SE, axis=AX.X,
                                    op=ALU.max).then_inc(ve.h, 1); k += 1
                    v.wait_ge(ve.h, k)
                    v.tensor_scalar(SE, SE, srow[:, c0 + 4:c0 + 5],
                                    srow[:, c0 + 3:c0 + 4],
                                    op0=ALU.is_equal, op1=ALU.mult).then_inc(ve.h, 1)
                    k += 1
            ve.n += 9 * ntt

            @E.blk.scalar
            def _(s, ve0=ve0, ac0=ac0):
                for tt in range(ntt):
                    c0 = 7 * tt
                    L = logi[:, tt * E8:(tt + 1) * E8]
                    s.wait_ge(ve.h, ve0 + 9 * tt + 2)
                    s.activation(out=L, in_=L, func=AF.Exp,
                                 bias=srow[:, c0:c0 + 1],
                                 accum_out=srow[:, c0 + 1:c0 + 2]).then_inc(ac_.h, 1)
            ac_.n += ntt

            pe1, ve1 = pe.n, ve.n
            @E.blk.tensor
            def _(t_, ve1=ve1):
                t_.wait_ge(ve.h, ve1)
                for tt in range(ntt):
                    t_.transpose(P[0:E8, 3584 + 128 * tt:3584 + 128 * (tt + 1)],
                                 seltm[:, tt * E8:(tt + 1) * E8],
                                 cst["idf"][:, :]).then_inc(pe.h, 1)
            pe.n += ntt

            @E.blk.vector
            def _(v, pe1=pe1):
                v.wait_ge(pe.h, pe1 + ntt)
                v.tensor_copy(selT[0:E8, :], P[0:E8, 3584:3584 + TT]).then_inc(ve.h, 1)
            ve.n += 1
            sel_done = ve.n

            d1b = [s.n for s in d1]
            d2b = [s.n for s in d2]
            peb, acb, veb = pe.n, ac_.n, ve.n

            @E.blk.sync
            def _(q, d1b=d1b, peb=peb):
                for e in range(E8):
                    for s in range(NS):
                        i = e * NS + s
                        if i >= 2:
                            i2 = i - 2
                            q.wait_ge(pe.h, peb + (i2 // NS) * PE_E + 4 * (i2 % NS) + 4)
                        else:
                            q.wait_ge(pe.h, peb)
                        src = w1_bf[e].rearrange("(c p) x -> p c x", p=128)
                        for half in range(2):
                            q.dma_start(
                                out=w1r[:, (i % 2) * NCH * 512 + half * 4 * 512:
                                        (i % 2) * NCH * 512 + (half + 1) * 4 * 512],
                                in_=src[:, 4 * half:4 * half + 4, 512 * s:512 * (s + 1)],
                            ).then_inc(d1[i % 2].h, 16)
            for i in range(E8 * NS):
                d1[i % 2].n += 32

            @E.blk.gpsimd
            def _(q, d2b=d2b, peb=peb):
                for e in range(E8):
                    for j in range(NP2):
                        i = e * NP2 + j
                        if i >= 2:
                            i2 = i - 2
                            e2, j2 = i2 // NP2, i2 % NP2
                            f2 = 4 * j2 + 3
                            tgt = (peb + e2 * PE_E + 34 + f2 if f2 < NF - 1
                                   else peb + e2 * PE_E + PE_E)
                            q.wait_ge(pe.h, tgt)
                        else:
                            q.wait_ge(pe.h, peb)
                        src = w2_bf[e][512 * j:512 * (j + 1), :].rearrange(
                            "(f p) x -> p f x", p=128)
                        q.dma_start(out=w2r[:, (i % 2) * 4 * D_MODEL:
                                            ((i % 2) + 1) * 4 * D_MODEL],
                                    in_=src).then_inc(d2[i % 2].h, 16)
            for i in range(E8 * NP2):
                d2[i % 2].n += 16

            @E.blk.tensor
            def _(t_, g=g, l=l, d1b=d1b, d2b=d2b, peb=peb, acb=acb, veb=veb,
                  sel_done=sel_done):
                for e in range(E8):
                    for f in range(NF):
                        s = f // 4
                        i = e * NS + s
                        t_.wait_ge(d1[i % 2].h, d1b[i % 2] + 32 * (i // 2 + 1))
                        if e > 0 and f < 2:
                            t_.wait_ge(ve.h, veb + (e - 1) * VE_E + 2 * f + 3)
                        if f >= 2:
                            t_.wait_ge(ac_.h, acb + e * NF + f - 1)
                        off = (i % 2) * NCH * 512 + (f % 4) * 128
                        for c in range(NCH):
                            mm = t_.matmul(
                                P[:, (f % 2) * 512:(f % 2 + 1) * 512],
                                w1r[:, off + 512 * c:off + 512 * c + 128],
                                xn[:, n_tok * c + g * TT:n_tok * c + (g + 1) * TT],
                                start=(c == 0), stop=(c == NCH - 1))
                            if c == NCH - 1:
                                mm.then_inc(pe.h, 1)
                    t_.wait_ge(ve.h, sel_done)
                    if e > 0:
                        t_.wait_ge(ve.h, veb + (e - 1) * VE_E + VE_E)
                    t_.matmul(P[:, 7 * 512:8 * 512],
                              cst["oh"][0:E8, 128 * e:128 * (e + 1)],
                              selT[0:E8, :], start=True, stop=True).then_inc(pe.h, 1)
                    for f in range(NF):
                        j = f // 4
                        i = e * NP2 + j
                        t_.wait_ge(d2[i % 2].h, d2b[i % 2] + 16 * (i // 2 + 1))
                        if f == 0:
                            t_.wait_ge(ac_.h, acb + (e + 1) * NF)
                        if e > 0 and f == 0:
                            for m in range(2, 7):
                                t_.wait_ge(ve.h, veb + (e - 1) * VE_E + 2 * m + 3)
                        w2off = (i % 2) * 4 * D_MODEL + (f % 4) * D_MODEL
                        for m in range(8):
                            if m == 7 and f == 0:
                                t_.wait_ge(ve.h, veb + e * VE_E + 1)
                            mm = t_.matmul(
                                P[:, m * 512:(m + 1) * 512],
                                w2r[:, w2off + 128 * m:w2off + 128 * (m + 1)],
                                hp[:, f * TT:(f + 1) * TT],
                                start=(f == 0), stop=(f == NF - 1),
                                skip_group_check=True)
                            if (f < NF - 1 and m == 7) or f == NF - 1:
                                mm.then_inc(pe.h, 1)
            pe.n += E8 * PE_E

            @E.blk.scalar
            def _(s, peb=peb, acb=acb, act=AF.Silu):
                for e in range(E8):
                    for f in range(NF):
                        s.wait_ge(pe.h, peb + e * PE_E + f + 1)
                        s.wait_ge(ac_.h, acb + ((e - 1) * NF + f + 1 if e > 0 else 0))
                        s.activation(out=hp[:, f * TT:(f + 1) * TT],
                                     in_=P[:, (f % 2) * 512:(f % 2 + 1) * 512],
                                     func=act).then_inc(ac_.h, 1)
            ac_.n += E8 * NF

            @E.blk.vector
            def _(v, peb=peb, veb=veb):
                k = veb
                for e in range(E8):
                    v.wait_ge(pe.h, peb + e * PE_E + 33)
                    v.wait_ge(ve.h, k)
                    v.tensor_copy(selb[:, (e % 2) * TT:(e % 2 + 1) * TT],
                                  P[:, 7 * 512:8 * 512]).then_inc(ve.h, 1); k += 1
                    sb = selb[:, (e % 2) * TT:(e % 2 + 1) * TT]
                    for m in range(8):
                        v.wait_ge(pe.h, peb + e * PE_E + 65 + m)
                        v.wait_ge(ve.h, k)
                        if e == 0:
                            v.tensor_tensor(tmp[:, :], P[:, m * 512:(m + 1) * 512],
                                            sb, op=ALU.mult).then_inc(ve.h, 1); k += 1
                            v.wait_ge(ve.h, k)
                            v.tensor_copy(acc[:, m * TT:(m + 1) * TT],
                                          tmp[:, :]).then_inc(ve.h, 1); k += 1
                        else:
                            v.tensor_tensor(tmp[:, :], P[:, m * 512:(m + 1) * 512],
                                            sb, op=ALU.mult).then_inc(ve.h, 1); k += 1
                            v.wait_ge(ve.h, k)
                            v.tensor_tensor(acc[:, m * TT:(m + 1) * TT],
                                            acc[:, m * TT:(m + 1) * TT], tmp[:, :],
                                            op=ALU.add).then_inc(ve.h, 1); k += 1
            ve.n += E8 * VE_E

            @E.blk.vector
            def _(v, g=g, l=l, last=(l == nlayers - 1)):
                k = ve.n
                for c in range(NCH):
                    dst = slice(n_tok * c + g * TT, n_tok * c + (g + 1) * TT)
                    v.wait_ge(ve.h, k)
                    if last and debug == "acc":
                        v.tensor_copy(x_res[:, dst],
                                      acc[:, c * TT:(c + 1) * TT]).then_inc(ve.h, 1)
                    elif last:
                        v.scalar_tensor_tensor(x_res[:, dst], acc[:, c * TT:(c + 1) * TT],
                                               res_scale, x_res[:, dst],
                                               op0=ALU.mult, op1=ALU.add).then_inc(ve.h, 1)
                    else:
                        v.tensor_copy(xn[:, dst],
                                      acc[:, c * TT:(c + 1) * TT]).then_inc(ve.h, 1)
                    k += 1
            ve.n += NCH
            prev_step_ve = ve.n


def build_ffn_l0(n_tok=T_CORE, nlayers=4, nexperts=16, stages=("cw", "ln", "ffn"),
                 debug=None):
    nc = bass.Bass("TRN2", target_bir_lowering=False)
    x = nc.dram_tensor("x", [n_tok, D_MODEL], F32, kind="ExternalInput").ap()
    routers = nc.dram_tensor("routers", [4, D_MODEL, 16], F32, kind="ExternalInput").ap()
    w1 = nc.dram_tensor("w1", [nexperts, D_MODEL, FF], F32, kind="ExternalInput").ap()
    w2 = nc.dram_tensor("w2", [nexperts, FF, D_MODEL], F32, kind="ExternalInput").ap()
    idf_d = nc.dram_tensor("c_idf", [128, 128], F32, kind="ExternalInput").ap()
    onesr_d = nc.dram_tensor("c_ones_row", [1, 128], F32, kind="ExternalInput").ap()
    onesc_d = nc.dram_tensor("c_ones_col", [128, 1], F32, kind="ExternalInput").ap()
    oh_d = nc.dram_tensor("c_oh", [16, 16 * 128], F32, kind="ExternalInput").ap()
    rank_d = nc.dram_tensor("c_rank16", [128, 16], F32, kind="ExternalInput").ap()
    y = nc.dram_tensor("y", [n_tok, D_MODEL], F32, kind="ExternalOutput").ap()

    w1_bf = nc.dram_tensor("w1_bf", [nexperts, D_MODEL, FF], BF16)
    w2_bf = nc.dram_tensor("w2_bf", [nexperts, FF, D_MODEL], BF16)

    with contextlib.ExitStack() as st:
        blk = st.enter_context(nc.Block())
        E = Ctx(nc, blk, st)
        x_res = E.sbuf("x_res", [128, NCH * n_tok], F32)
        xn = E.sbuf("xn", [128, NCH * n_tok], BF16)
        idf = E.sbuf("idf", [128, 128], F32)
        ones_row = E.sbuf("ones_row", [1, 128], F32)
        ones_col = E.sbuf("ones_col", [128, 1], F32)
        oh = E.sbuf("oh", [16, 16 * 128], F32)
        rank = E.sbuf("rank", [128, 16], F32)
        routers_s = E.sbuf("routers_s", [128, 4 * NCH * 16], BF16)

        d0 = E.S("d_init")

        @E.blk.gpsimd
        def _(g):
            g.dma_start(out=idf[:, :], in_=idf_d[:, :]).then_inc(d0.h, 16)
            g.dma_start(out=ones_row[:, :], in_=onesr_d[:, :]).then_inc(d0.h, 16)
            g.dma_start(out=ones_col[:, :], in_=onesc_d[:, :]).then_inc(d0.h, 16)
            g.dma_start(out=oh[:, :], in_=oh_d[:, :]).then_inc(d0.h, 16)
            g.dma_start(out=rank[:, :], in_=rank_d[:, :]).then_inc(d0.h, 16)
            for l in range(4):
                for c in range(NCH):
                    g.dma_start(out=routers_s[:, (l * NCH + c) * 16:(l * NCH + c + 1) * 16],
                                in_=routers[l][128 * c:128 * (c + 1), :]).then_inc(d0.h, 16)
            g.wait_ge(d0.h, 16 * 37)
        d0.n += 16 * 37

        with contextlib.ExitStack() as sc0:
            emit_load_x(E, sc0, x, x_res, n_tok, idf)
            if "cw" in stages:
                emit_convert_w(E, sc0, w1.rearrange("e a b -> (e a) b"),
                               w1_bf.ap().rearrange("e a b -> (e a) b"),
                               nexperts * D_MODEL, FF, tag="w1")
                emit_convert_w(E, sc0, w2.rearrange("e a b -> (e a) b"),
                               w2_bf.ap().rearrange("e a b -> (e a) b"),
                               nexperts * FF, D_MODEL, tag="w2")
        emit_barrier(E)
        if "ln" in stages:
            with contextlib.ExitStack() as sc1:
                emit_ln(E, sc1, x_res, xn, n_tok, ones_row, ones_col)
            emit_barrier(E)
            if debug == "xn":
                vx = E.S("ve_dbg")
                @E.blk.vector
                def _(v):
                    k = vx.n
                    for c in range(NCH):
                        v.tensor_copy(x_res[:, n_tok * c:n_tok * (c + 1)],
                                      xn[:, n_tok * c:n_tok * (c + 1)]).then_inc(vx.h, 1)
                vx.n += NCH
                emit_barrier(E)
        if "ffn" in stages:
            with contextlib.ExitStack() as sc2:
                cst = {"idf": idf, "oh": oh, "rank": rank}
                emit_unipool_ffn(E, sc2, x_res, xn, n_tok, cst, routers_s,
                                 w1_bf.ap(), w2_bf.ap(), nlayers=nlayers,
                                 nexperts=nexperts, debug=debug)
            emit_barrier(E)
        with contextlib.ExitStack() as sc3:
            emit_store_x(E, sc3, x_res, y, n_tok, idf)
    return nc


def emit_attention(E, scope, xn, attres, n_q, n_kv, cst, kind, wq_bf, wo_bf,
                   prm, tag="a"):
    NH = 16
    NHP = NH // 2
    nqt = n_q // 128
    nkc = n_kv // 128
    l1 = (kind == "l1")

    att = E.sbuf(tag + "_att", [128, 8 * n_q], BF16, scope)
    hstack = contextlib.ExitStack()
    whp = E.sbuf(tag + "_whp", [128, 24 * 128], BF16, hstack)
    qbuf = E.sbuf(tag + "_q", [128, n_q], BF16, hstack)
    kbuf = E.sbuf(tag + "_k", [128, n_kv], BF16, hstack)
    vbuf = E.sbuf(tag + "_v", [128, n_kv], BF16, hstack)
    sc_s = E.sbuf(tag + "_sc", [128, n_kv], F32, hstack)
    pb_s = E.sbuf(tag + "_pb", [128, n_kv], BF16, hstack)
    pT = E.sbuf(tag + "_pT", [128, n_kv], BF16, hstack)
    stat = E.sbuf(tag + "_stat", [128, 8], F32, hstack)
    if not l1:
        ln1 = E.sbuf(tag + "_ln1", [128, 1024], F32, hstack)
        ln2 = E.sbuf(tag + "_ln2", [128, 1024], F32, hstack)
        rws = E.sbuf(tag + "_rws", [2, 2 * n_kv], F32, hstack)
    else:
        ln1 = ln2 = rws = None
    Pa = E.psum(tag + "_Pa", [128, 2048], F32, hstack)
    Ptb = E.psum(tag + "_Ptb", [128, 2048], BF16, hstack)
    Pav = E.psum(tag + "_Pav", [128, 512], F32, hstack)
    Pln = E.psum(tag + "_Pln", [128, 512], F32, hstack)

    dW = [E.S("d_aw0"), E.S("d_aw1")]
    pe = E.S("pe_at")
    ve = E.S("ve_at")
    ac = E.S("ac_at")

    def wofs(j, hp):
        return j * 1024 + 128 * hp

    hw = min(2048, n_kv)
    nhalf = n_kv // hw
    for hp in range(NHP):
        dWb = [s.n for s in dW]
        peb, veb, acb = pe.n, ve.n, ac.n

        @E.blk.sync
        def _(q_, hp=hp, peb=peb):
            q_.wait_ge(pe.h, peb)
            for j in range(3):
                for c in range(NCH):
                    fo = wofs(j, hp)
                    if l1:
                        src = wq_bf[fo:fo + 128, 128 * c:128 * (c + 1)]
                    else:
                        src = wq_bf[128 * c:128 * (c + 1), fo:fo + 128]
                    q_.dma_start(out=whp[:, (j * 8 + c) * 128:(j * 8 + c + 1) * 128],
                                 in_=src).then_inc(dW[hp % 2].h, 16)
        dW[hp % 2].n += 16 * 24

        if l1:
            @E.blk.tensor
            def _(t_, hp=hp, dWb=dWb, veb=veb):
                t_.wait_ge(dW[hp % 2].h, dWb[hp % 2] + 16 * 24)
                t_.wait_ge(ve.h, veb)
                for i in range(24):
                    if i % 8 == 0 and i >= 16:
                        t_.wait_ge(ve.h, veb + i - 8)
                    b = ((i // 8) % 2) * 1024 + (i % 8) * 128
                    t_.transpose(Ptb[:, b:b + 128],
                                 whp[:, i * 128:(i + 1) * 128],
                                 cst["idb"][:, :]).then_inc(pe.h, 1)
            pe.n += 24

            @E.blk.vector
            def _(v, peb=peb, veb=veb):
                for i in range(24):
                    b = ((i // 8) % 2) * 1024 + (i % 8) * 128
                    if i % 8 == 0:
                        v.wait_ge(pe.h, peb + i + 8)
                    v.wait_ge(ve.h, veb + i)
                    v.tensor_copy(whp[:, i * 128:(i + 1) * 128],
                                  Ptb[:, b:b + 128]).then_inc(ve.h, 1)
            ve.n += 24

        qkv_peb, qkv_veb = pe.n, ve.n
        qkv_dwb = [s.n for s in dW]

        nqk = n_q // 512 + n_kv // 512
        @E.blk.tensor
        def _(t_, hp=hp, qkv_dwb=qkv_dwb, qkv_veb=qkv_veb):
            if not l1:
                t_.wait_ge(dW[hp % 2].h, qkv_dwb[hp % 2])
            t_.wait_ge(ve.h, qkv_veb)
            n = 0
            for j, nt in ((0, n_q // 512), (1, n_kv // 512)):
                for jt in range(nt):
                    if n >= 4:
                        t_.wait_ge(ve.h, qkv_veb + n - 3)
                    for c in range(NCH):
                        mm = t_.matmul(
                            Pa[:, (n % 4) * 512:(n % 4) * 512 + 512],
                            whp[:, (j * 8 + c) * 128:(j * 8 + c + 1) * 128],
                            xn[:, n_kv * c + 512 * jt:n_kv * c + 512 * (jt + 1)],
                            start=(c == 0), stop=(c == NCH - 1))
                        if c == NCH - 1:
                            mm.then_inc(pe.h, 1)
                    n += 1
            for kt in range(nkc):
                if n >= 4:
                    t_.wait_ge(ve.h, qkv_veb + n - 3)
                for c in range(NCH):
                    mm = t_.matmul(
                        Pa[:, (n % 4) * 512:(n % 4) * 512 + 128],
                        xn[:, n_kv * c + 128 * kt:n_kv * c + 128 * (kt + 1)],
                        whp[:, (16 + c) * 128:(16 + c + 1) * 128],
                        start=(c == 0), stop=(c == NCH - 1))
                    if c == NCH - 1:
                        mm.then_inc(pe.h, 1)
                n += 1
        pe.n += nqk + nkc

        @E.blk.vector
        def _(v, qkv_peb=qkv_peb, qkv_veb=qkv_veb):
            n = 0
            for j, nt, buf in ((0, n_q // 512, qbuf), (1, n_kv // 512, kbuf)):
                for jt in range(nt):
                    v.wait_ge(pe.h, qkv_peb + n + 1)
                    v.wait_ge(ve.h, qkv_veb + n)
                    v.tensor_copy(buf[:, 512 * jt:512 * (jt + 1)],
                                  Pa[:, (n % 4) * 512:(n % 4) * 512 + 512]
                                  ).then_inc(ve.h, 1)
                    n += 1
            for kt in range(nkc):
                v.wait_ge(pe.h, qkv_peb + n + 1)
                v.wait_ge(ve.h, qkv_veb + n)
                v.tensor_copy(vbuf[:, 128 * kt:128 * (kt + 1)],
                              Pa[:, (n % 4) * 512:(n % 4) * 512 + 128]).then_inc(ve.h, 1)
                n += 1
        ve.n += nqk + nkc

        if l1:
            emit_rope(E, hp, qbuf, kbuf, n_q, n_kv, cst, prm, Pln, pe, ve, ac, tag)
        else:
            emit_qkln(E, hp, qbuf, kbuf, n_q, n_kv, cst, prm, Pln, pe, ve, ac,
                      ln1, ln2, rws, tag)

        for h2 in range(2):
            po = 64 * h2
            for qt in range(nqt):
                sb, vb2, ab = pe.n, ve.n, ac.n
                @E.blk.tensor
                def _(t_, hp=hp, h2=h2, qt=qt, sb=sb, vb2=vb2):
                    for half in range(nhalf):
                        t_.wait_ge(ve.h, vb2 + half)
                        for kc in range(hw // 512):
                            mm = t_.matmul(
                                Pa[:, kc * 512:kc * 512 + 512],
                                qbuf[po:po + 64, 128 * qt:128 * (qt + 1)],
                                kbuf[po:po + 64,
                                     hw * half + 512 * kc:hw * half + 512 * (kc + 1)],
                                start=True, stop=True)
                            mm.then_inc(pe.h, 1)
                @E.blk.vector
                def _(v, sb=sb, vb2=vb2):
                    for half in range(nhalf):
                        v.wait_ge(pe.h, sb + (hw // 512) * (half + 1))
                        v.wait_ge(ve.h, vb2 + half)
                        v.tensor_scalar(sc_s[:, hw * half:hw * (half + 1)],
                                        Pa[:, 0:hw], 1.0, None, op0=ALU.mult,
                                        op1=ALU.max,
                                        accum_out=stat[:, half:half + 1]).then_inc(ve.h, 1)
                pe.n += (hw // 512) * nhalf
                ve.n += nhalf
                vb3 = ve.n

                @E.blk.vector
                def _(v, vb3=vb3, nhalf=nhalf):
                    v.wait_ge(ve.h, vb3)
                    m = stat[:, 0:1]
                    if nhalf > 1:
                        v.tensor_tensor(stat[:, 4:5], stat[:, 0:1], stat[:, 1:2],
                                        op=ALU.max).then_inc(ve.h, 1)
                        m = stat[:, 4:5]
                    else:
                        v.tensor_copy(stat[:, 4:5], stat[:, 0:1]).then_inc(ve.h, 1)
                    v.wait_ge(ve.h, vb3 + 1)
                    v.tensor_scalar(stat[:, 5:6], stat[:, 4:5], -1.0, None,
                                    op0=ALU.mult).then_inc(ve.h, 1)
                ve.n += 2
                vb4 = ve.n

                @E.blk.scalar
                def _(s, vb4=vb4, ab=ab):
                    s.wait_ge(ve.h, vb4)
                    s.activation(out=sc_s[:, 0:n_kv], in_=sc_s[:, 0:n_kv],
                                 func=AF.Exp, bias=stat[:, 5:6],
                                 accum_out=stat[:, 6:7]).then_inc(ac.h, 1)
                ac.n += 1

                @E.blk.vector
                def _(v, ab=ab, vb4=vb4):
                    v.wait_ge(ac.h, ab + 1)
                    v.reciprocal(stat[:, 7:8], stat[:, 6:7]).then_inc(ve.h, 1)
                    v.wait_ge(ve.h, vb4 + 1)
                    v.tensor_scalar(pb_s[:, 0:n_kv], sc_s[:, 0:n_kv],
                                    stat[:, 7:8], None, op0=ALU.mult).then_inc(ve.h, 1)
                ve.n += 2
                vb5 = ve.n
                pb2 = pe.n

                @E.blk.tensor
                def _(t_, pb2=pb2, vb5=vb5):
                    t_.wait_ge(ve.h, vb5)
                    for i in range(nkc):
                        if i % 8 == 0 and i >= 16:
                            t_.wait_ge(ve.h, vb5 + i - 8)
                        b = ((i // 8) % 2) * 1024 + (i % 8) * 128
                        t_.transpose(Ptb[:, b:b + 128],
                                     pb_s[:, 128 * i:128 * (i + 1)],
                                     cst["idb"][:, :]).then_inc(pe.h, 1)
                pe.n += nkc

                @E.blk.vector
                def _(v, pb2=pb2, vb5=vb5):
                    for i in range(nkc):
                        b = ((i // 8) % 2) * 1024 + (i % 8) * 128
                        if i % 8 == 0:
                            v.wait_ge(pe.h, pb2 + min(i + 8, nkc))
                        v.wait_ge(ve.h, vb5 + i)
                        v.tensor_copy(pT[:, 128 * i:128 * (i + 1)],
                                      Ptb[:, b:b + 128]).then_inc(ve.h, 1)
                ve.n += nkc
                vb6 = ve.n
                pb3 = pe.n

                @E.blk.tensor
                def _(t_, vb6=vb6, po=po):
                    for i in range(nkc):
                        t_.wait_ge(ve.h, vb6 - nkc + i + 1)
                        mm = t_.matmul(Pav[0:64, 0:128],
                                       vbuf[:, 128 * i:128 * (i + 1)][:, po:po + 64],
                                       pT[:, 128 * i:128 * (i + 1)],
                                       start=(i == 0), stop=(i == nkc - 1))
                        if i == nkc - 1:
                            mm.then_inc(pe.h, 1)
                pe.n += 1

                @E.blk.vector
                def _(v, pb3=pb3, hp=hp, h2=h2, qt=qt, po=po):
                    v.wait_ge(pe.h, pb3 + 1)
                    hh = 2 * hp + h2
                    ch, op = hh // 2, 64 * (hh % 2)
                    v.tensor_copy(att[op:op + 64, n_q * ch + 128 * qt:
                                      n_q * ch + 128 * (qt + 1)],
                                  Pav[0:64, 0:128]).then_inc(ve.h, 1)
                ve.n += 1

    hstack.close()
    emit_out_proj(E, scope, att, attres, n_q, cst, l1, wo_bf, prm, pe, ve, ac, tag)


def emit_qkln(E, hp, qbuf, kbuf, n_q, n_kv, cst, prm, Pln, pe, ve, ac,
              ln1, ln2, rws, tag):
    for which, buf, n, gcol, bcol in (("q", qbuf, n_q, prm["qg"], prm["qb"]),
                                      ("k", kbuf, n_kv, prm["kg"], prm["kb"])):
        nj = n // 512
        peb, veb, acb = pe.n, ve.n, ac.n

        @E.blk.scalar
        def _(s, buf=buf, nj=nj, peb=peb, acb=acb):
            s.wait_ge(ve.h, ve.n)
            for j in range(nj):
                if j >= 2:
                    s.wait_ge(pe.h, peb + 2 * (j - 1))
                s.activation(out=ln1[:, (j % 2) * 512:(j % 2 + 1) * 512],
                             in_=buf[:, 512 * j:512 * (j + 1)],
                             func=AF.Square).then_inc(ac.h, 1)
        ac.n += nj

        @E.blk.tensor
        def _(t_, buf=buf, nj=nj, peb=peb, acb=acb, veb=veb):
            t_.wait_ge(ve.h, veb)
            for j in range(nj):
                if j > 0:
                    t_.wait_ge(ve.h, veb + 2 * j)
                t_.matmul(Pln[0:2, 0:512], cst["hmask_b"][:, 0:2],
                          buf[:, 512 * j:512 * (j + 1)],
                          start=True, stop=True).then_inc(pe.h, 1)
                t_.wait_ge(ac.h, acb + j + 1)
                t_.wait_ge(ve.h, veb + 2 * j + 1)
                t_.matmul(Pln[32:34, 0:512], cst["hmask_f"][:, 0:2],
                          ln1[:, (j % 2) * 512:(j % 2 + 1) * 512],
                          start=True, stop=True).then_inc(pe.h, 1)
        pe.n += 2 * nj

        @E.blk.vector
        def _(v, n=n, nj=nj, peb=peb, veb=veb):
            k = veb
            for j in range(nj):
                v.wait_ge(pe.h, peb + 2 * j + 1)
                v.tensor_scalar(rws[0:2, 512 * j:512 * (j + 1)], Pln[0:2, 0:512],
                                1.0 / 64, None, op0=ALU.mult).then_inc(ve.h, 1)
                v.wait_ge(pe.h, peb + 2 * j + 2)
                v.tensor_scalar(rws[0:2, n + 512 * j:n + 512 * (j + 1)],
                                Pln[32:34, 0:512], 1.0 / 64, None,
                                op0=ALU.mult).then_inc(ve.h, 1)
                k += 2
            for j in range(nj):
                sl = slice(512 * j, 512 * (j + 1))
                sln = slice(n + 512 * j, n + 512 * (j + 1))
                v.wait_ge(ve.h, k)
                v.tensor_tensor(ln2[0:2, 0:512], rws[0:2, sl], rws[0:2, sl],
                                op=ALU.mult).then_inc(ve.h, 1)
                v.wait_ge(ve.h, k + 1)
                v.tensor_tensor(rws[0:2, sln], rws[0:2, sln], ln2[0:2, 0:512],
                                op=ALU.subtract).then_inc(ve.h, 1)
                v.wait_ge(ve.h, k + 2)
                v.tensor_scalar(rws[0:2, sln], rws[0:2, sln], EPS, None,
                                op0=ALU.add).then_inc(ve.h, 1)
                k += 3
        ve.n += 5 * nj
        acb2 = ac.n

        @E.blk.scalar
        def _(s, n=n):
            s.wait_ge(ve.h, ve.n)
            s.activation(out=rws[0:2, n:2 * n], in_=rws[0:2, n:2 * n],
                         func=AF.Sqrt).then_inc(ac.h, 1)
        ac.n += 1

        @E.blk.vector
        def _(v, n=n, acb2=acb2):
            v.wait_ge(ac.h, acb2 + 1)
            v.reciprocal(rws[0:2, n:2 * n], rws[0:2, n:2 * n]).then_inc(ve.h, 1)
            v.wait_ge(ve.h, ve.n + 1)
            v.tensor_tensor(rws[0:2, 0:n], rws[0:2, 0:n], rws[0:2, n:2 * n],
                            op=ALU.mult).then_inc(ve.h, 1)
        ve.n += 2
        peb2, veb2 = pe.n, ve.n

        @E.blk.tensor
        def _(t_, n=n, nj=nj, peb2=peb2, veb2=veb2):
            t_.wait_ge(ve.h, veb2)
            for j in range(nj):
                if j > 0:
                    t_.wait_ge(ve.h, veb2 + 4 * j)
                t_.matmul(Pln[:, 0:512], cst["sel2"][0:2, :],
                          rws[0:2, n + 512 * j:n + 512 * (j + 1)],
                          start=True, stop=True).then_inc(pe.h, 1)
                t_.wait_ge(ve.h, veb2 + 4 * j + 1)
                t_.matmul(Pln[:, 0:512], cst["sel2"][0:2, :],
                          rws[0:2, 512 * j:512 * (j + 1)],
                          start=True, stop=True).then_inc(pe.h, 1)
                t_.wait_ge(ve.h, veb2 + 4 * j + 2)
        pe.n += 2 * nj

        @E.blk.vector
        def _(v, buf=buf, n=n, nj=nj, gcol=gcol, bcol=bcol, peb2=peb2, veb2=veb2):
            k = veb2
            for j in range(nj):
                sl = slice(512 * j, 512 * (j + 1))
                v.wait_ge(pe.h, peb2 + 2 * j + 1)
                v.wait_ge(ve.h, k)
                v.tensor_scalar(ln1[:, 0:512], Pln[:, 0:512], gcol[:, 0:1], None,
                                op0=ALU.mult).then_inc(ve.h, 1)
                v.wait_ge(pe.h, peb2 + 2 * j + 2)
                v.wait_ge(ve.h, k + 1)
                v.tensor_scalar(ln2[:, 0:512], Pln[:, 0:512], gcol[:, 0:1],
                                bcol[:, 0:1], op0=ALU.mult,
                                op1=ALU.subtract).then_inc(ve.h, 1)
                v.wait_ge(ve.h, k + 2)
                v.tensor_tensor(ln2[:, 512:1024], buf[:, sl], ln1[:, 0:512],
                                op=ALU.mult).then_inc(ve.h, 1)
                v.wait_ge(ve.h, k + 3)
                v.tensor_tensor(buf[:, sl], ln2[:, 512:1024], ln2[:, 0:512],
                                op=ALU.subtract).then_inc(ve.h, 1)
                k += 4
        ve.n += 4 * nj


def emit_rope(E, hp, qbuf, kbuf, n_q, n_kv, cst, prm, Pln, pe, ve, ac, tag):
    ln1 = prm["scr1"]
    for which, buf, n in (("q", qbuf, n_q), ("k", kbuf, n_kv)):
        nj = n // 512
        peb, veb = pe.n, ve.n
        stride = 5 if which == "q" else 4
        @E.blk.tensor
        def _(t_, buf=buf, n=n, nj=nj, peb=peb, veb=veb, stride=stride):
            t_.wait_ge(ve.h, veb)
            for j in range(nj):
                if j > 0:
                    t_.wait_ge(ve.h, veb + stride * (j - 1) + 2)
                t_.matmul(Pln[:, 0:512], cst["prot2"][:, :],
                          buf[:, 512 * j:512 * (j + 1)],
                          start=True, stop=True).then_inc(pe.h, 1)
        pe.n += nj

        @E.blk.vector
        def _(v, which=which, buf=buf, n=n, nj=nj, peb=peb, veb=veb):
            k = veb
            for j in range(nj):
                sl = slice(512 * j, 512 * (j + 1))
                v.wait_ge(ve.h, k)
                v.tensor_tensor(ln1[:, 0:512], buf[:, sl],
                                prm["cos2"][:, sl], op=ALU.mult).then_inc(ve.h, 1)
                v.wait_ge(pe.h, peb + j + 1)
                v.wait_ge(ve.h, k + 1)
                v.tensor_tensor(ln1[:, 512:1024], Pln[:, 0:512],
                                prm["sin2"][:, sl], op=ALU.mult).then_inc(ve.h, 1)
                v.wait_ge(ve.h, k + 2)
                v.tensor_tensor(ln1[:, 0:512], ln1[:, 0:512], ln1[:, 512:1024],
                                op=ALU.add).then_inc(ve.h, 1)
                v.wait_ge(ve.h, k + 3)
                if which == "q":
                    v.tensor_tensor(ln1[:, 0:512], ln1[:, 0:512],
                                    prm["fenc2"][:, sl], op=ALU.add).then_inc(ve.h, 1)
                    v.wait_ge(ve.h, k + 4)
                    v.tensor_scalar(buf[:, sl], ln1[:, 0:512], 0.125, None,
                                    op0=ALU.mult).then_inc(ve.h, 1)
                    k += 5
                else:
                    v.tensor_tensor(buf[:, sl], ln1[:, 0:512],
                                    prm["fenc2"][:, sl], op=ALU.add).then_inc(ve.h, 1)
                    k += 4
        ve.n += (5 if which == "q" else 4) * nj


def emit_out_proj(E, scope, att, attres, n_q, cst, l1, wo_bf, prm, pe, ve, ac, tag):
    wop = E.sbuf(tag + "_wop", [128, 64 * 128], BF16, scope)
    Pp = E.psum(tag + "_Pp", [128, 2048], F32, scope)
    Ptp = E.psum(tag + "_Ptp", [128, 2048], BF16, scope)
    dP = E.S("d_apw")
    peb, veb, db = pe.n, ve.n, dP.n

    @E.blk.gpsimd
    def _(g):
        g.wait_ge(ve.h, veb)
        g.wait_ge(pe.h, peb)
        for c in range(NCH):
            for m in range(NCH):
                i = c * 8 + m
                src = (wo_bf[128 * m:128 * (m + 1), 128 * c:128 * (c + 1)] if l1
                       else wo_bf[128 * c:128 * (c + 1), 128 * m:128 * (m + 1)])
                g.dma_start(out=wop[:, i * 128:(i + 1) * 128],
                            in_=src).then_inc(dP.h, 16)
    dP.n += 16 * 64

    if l1:
        @E.blk.tensor
        def _(t_, db=db, veb=veb):
            t_.wait_ge(dP.h, db + 16 * 64)
            t_.wait_ge(ve.h, veb)
            for i in range(64):
                if i % 8 == 0 and i >= 16:
                    t_.wait_ge(ve.h, veb + i - 8)
                b = ((i // 8) % 2) * 1024 + (i % 8) * 128
                t_.transpose(Ptp[:, b:b + 128], wop[:, i * 128:(i + 1) * 128],
                             cst["idb"][:, :]).then_inc(pe.h, 1)
        pe.n += 64

        @E.blk.vector
        def _(v, peb=peb, veb=veb):
            for i in range(64):
                b = ((i // 8) % 2) * 1024 + (i % 8) * 128
                if i % 8 == 0:
                    v.wait_ge(pe.h, peb + min(i + 8, 64))
                v.wait_ge(ve.h, veb + i)
                v.tensor_copy(wop[:, i * 128:(i + 1) * 128],
                              Ptp[:, b:b + 128]).then_inc(ve.h, 1)
        ve.n += 64

    pj_pe, pj_ve = pe.n, ve.n
    nt = n_q // 512

    @E.blk.tensor
    def _(t_, pj_pe=pj_pe, pj_ve=pj_ve, db=db):
        t_.wait_ge(ve.h, pj_ve)
        t_.wait_ge(pe.h, pj_pe)
        if not l1:
            t_.wait_ge(dP.h, db + 16 * 64)
        n = 0
        for m in range(NCH):
            for jt in range(nt):
                if n >= 4:
                    t_.wait_ge(ve.h, pj_ve + n - 3)
                for c in range(NCH):
                    mm = t_.matmul(Pp[:, (n % 4) * 512:(n % 4) * 512 + 512],
                                   wop[:, (c * 8 + m) * 128:(c * 8 + m + 1) * 128],
                                   att[:, n_q * c + 512 * jt:n_q * c + 512 * (jt + 1)],
                                   start=(c == 0), stop=(c == NCH - 1))
                    if c == NCH - 1:
                        mm.then_inc(pe.h, 1)
                n += 1
    pe.n += NCH * nt

    @E.blk.vector
    def _(v, pj_pe=pj_pe, pj_ve=pj_ve):
        n = 0
        for m in range(NCH):
            for jt in range(nt):
                v.wait_ge(pe.h, pj_pe + n + 1)
                v.wait_ge(ve.h, pj_ve + n)
                if l1:
                    v.tensor_scalar(attres[:, n_q * m + 512 * jt:n_q * m + 512 * (jt + 1)],
                                    Pp[:, (n % 4) * 512:(n % 4) * 512 + 512],
                                    prm["bout"][:, m:m + 1], None,
                                    op0=ALU.add).then_inc(ve.h, 1)
                else:
                    v.tensor_copy(attres[:, n_q * m + 512 * jt:n_q * m + 512 * (jt + 1)],
                                  Pp[:, (n % 4) * 512:(n % 4) * 512 + 512]).then_inc(ve.h, 1)
                n += 1
    ve.n += NCH * nt


def emit_store_attn(E, scope, attres, x_dram, y_dram, n_q, cst):
    ntile = n_q // 128
    xout = E.sbuf("sa_xout", [128, 2 * D_MODEL], F32, scope)
    xin = E.sbuf("sa_xin", [128, 2 * D_MODEL], F32, scope)
    pt = E.psum("sa_pt", [128, 2048], BF16, scope)
    d = [E.S("d_sa0"), E.S("d_sa1")]
    dx = [E.S("d_sax0"), E.S("d_sax1")]
    pe = E.S("pe_sa")
    ve = E.S("ve_sa")
    db = [s.n for s in d]
    dxb = [s.n for s in dx]
    peb, veb = pe.n, ve.n

    @E.blk.gpsimd
    def _(g):
        for t in range(ntile):
            if t >= 2:
                g.wait_ge(ve.h, veb + 8 * (t - 1))
            g.dma_start(out=xin[:, (t % 2) * D_MODEL:(t % 2 + 1) * D_MODEL],
                        in_=x_dram[128 * t:128 * (t + 1), :]).then_inc(dx[t % 2].h, 16)
    for t in range(ntile):
        dx[t % 2].n += 16

    @E.blk.tensor
    def _(t_):
        for t in range(ntile):
            for c in range(NCH):
                i = t * 8 + c
                if i % 8 == 0 and i >= 16:
                    t_.wait_ge(ve.h, veb + i - 8)
                b = ((i // 8) % 2) * 1024 + (i % 8) * 128
                t_.transpose(pt[:, b:b + 128],
                             attres[:, n_q * c + 128 * t:n_q * c + 128 * (t + 1)],
                             cst["idb"][:, :]).then_inc(pe.h, 1)
    pe.n += 8 * ntile

    @E.blk.vector
    def _(v):
        for t in range(ntile):
            if t >= 2:
                v.wait_ge(d[t % 2].h, db[t % 2] + 16 * ((t - 2) // 2 + 1))
            v.wait_ge(dx[t % 2].h, dxb[t % 2] + 16 * (t // 2 + 1))
            for c in range(NCH):
                i = t * 8 + c
                if i % 8 == 0:
                    v.wait_ge(pe.h, peb + min(i + 8, 8 * ntile))
                b = ((i // 8) % 2) * 1024 + (i % 8) * 128
                v.tensor_tensor(xout[:, (t % 2) * D_MODEL + 128 * c:
                                     (t % 2) * D_MODEL + 128 * (c + 1)],
                                pt[:, b:b + 128],
                                xin[:, (t % 2) * D_MODEL + 128 * c:
                                    (t % 2) * D_MODEL + 128 * (c + 1)],
                                op=ALU.add).then_inc(ve.h, 1)
    ve.n += 8 * ntile

    @E.blk.sync
    def _(q):
        for t in range(ntile):
            q.wait_ge(ve.h, veb + 8 * (t + 1))
            q.dma_start(out=y_dram[128 * t:128 * (t + 1), :],
                        in_=xout[:, (t % 2) * D_MODEL:(t % 2 + 1) * D_MODEL]
                        ).then_inc(d[t % 2].h, 16)
        q.wait_ge(d[0].h, db[0] + 16 * ((ntile + 1) // 2))
        q.wait_ge(d[1].h, db[1] + 16 * (ntile // 2))
    for t in range(ntile):
        d[t % 2].n += 16


def rope_tables(n_kv, order):
    DH, NF_ = 64, 16
    pos = order.astype(np.float64)[:, None]
    freqs = 10000.0 ** (-np.arange(0, DH, 2, dtype=np.float64) / DH)
    ang = pos * freqs
    sin = np.repeat(np.sin(ang), 2, axis=1)
    cos = np.repeat(np.cos(ang), 2, axis=1)
    ff = np.arange(1, NF_ + 1, dtype=np.float64)
    fourier = np.concatenate([np.sin(pos * ff), np.cos(pos * ff)], 1)
    cos2 = np.broadcast_to(cos.T, (64, n_kv))
    cos2 = np.vstack([cos2, cos2]).astype(bfloat16)
    sin2 = np.broadcast_to(sin.T, (64, n_kv))
    sin2 = np.vstack([sin2, sin2]).astype(bfloat16)
    return {"c_cos2": np.ascontiguousarray(cos2),
            "c_sin2": np.ascontiguousarray(sin2),
            "c_fourier": np.ascontiguousarray(fourier.T.astype(np.float32))}


def build_attn(kind, n_q=T_CORE, n_kv=2 * T_CORE):
    l1 = (kind == "l1")
    nc = bass.Bass("TRN2", target_bir_lowering=False)
    x = nc.dram_tensor("x", [n_kv, D_MODEL], F32, kind="ExternalInput").ap()
    if l1:
        wq = nc.dram_tensor("w_qkv", [3 * D_MODEL, D_MODEL], F32, kind="ExternalInput").ap()
        wo = nc.dram_tensor("w_out", [D_MODEL, D_MODEL], F32, kind="ExternalInput").ap()
        wfp = nc.dram_tensor("w_fproj", [64, 32], F32, kind="ExternalInput").ap()
        bfp = nc.dram_tensor("b_fproj", [64], F32, kind="ExternalInput").ap()
        bout = nc.dram_tensor("b_out", [D_MODEL], F32, kind="ExternalInput").ap()
        cos2_d = nc.dram_tensor("c_cos2", [128, n_kv], BF16, kind="ExternalInput").ap()
        sin2_d = nc.dram_tensor("c_sin2", [128, n_kv], BF16, kind="ExternalInput").ap()
        four_d = nc.dram_tensor("c_fourier", [32, n_kv], F32, kind="ExternalInput").ap()
    else:
        wq = nc.dram_tensor("w_qkv", [D_MODEL, 3 * D_MODEL], F32, kind="ExternalInput").ap()
        wo = nc.dram_tensor("w_proj", [D_MODEL, D_MODEL], F32, kind="ExternalInput").ap()
        qg_d = nc.dram_tensor("q_gamma", [64], F32, kind="ExternalInput").ap()
        qb_d = nc.dram_tensor("q_beta", [64], F32, kind="ExternalInput").ap()
        kg_d = nc.dram_tensor("k_gamma", [64], F32, kind="ExternalInput").ap()
        kb_d = nc.dram_tensor("k_beta", [64], F32, kind="ExternalInput").ap()
    cdefs = {}
    for nm, shp in (("c_idf", [128, 128]), ("c_idb", [128, 128]),
                    ("c_ones_row", [1, 128]), ("c_ones_col", [128, 1]),
                    ("c_hmask_f", [128, 2]), ("c_hmask_b", [128, 2]),
                    ("c_sel2", [2, 128]), ("c_prot2", [128, 128])):
        dt = BF16 if nm in ("c_idb", "c_hmask_b", "c_prot2") else F32
        cdefs[nm] = nc.dram_tensor(nm, shp, dt, kind="ExternalInput").ap()
    y = nc.dram_tensor("y", [n_q, D_MODEL], F32, kind="ExternalOutput").ap()

    wq_bf = nc.dram_tensor("wq_bf", list(wq.shape), BF16)
    wo_bf = nc.dram_tensor("wo_bf", list(wo.shape), BF16)
    fenc_dr = nc.dram_tensor("fenc_dr", [64, n_kv], BF16) if l1 else None

    with contextlib.ExitStack() as st:
        blk = st.enter_context(nc.Block())
        E = Ctx(nc, blk, st)
        xn = E.sbuf("xn", [128, NCH * n_kv], BF16)
        attres = xn[:, 0:NCH * n_q]
        cst = {}
        cstbuf = {}
        d0 = E.S("d_init")
        for nm, ap in cdefs.items():
            dt = BF16 if nm in ("c_idb", "c_hmask_b", "c_prot2") else F32
            cstbuf[nm] = E.sbuf(nm + "_s", list(ap.shape), dt)
        prm = {}
        if l1:
            prm["cos2"] = E.sbuf("cos2_s", [128, n_kv], BF16)
            prm["sin2"] = E.sbuf("sin2_s", [128, n_kv], BF16)
            prm["fenc2"] = E.sbuf("fenc2_s", [128, n_kv], BF16)
            prm["scr1"] = E.sbuf("scr1_s", [128, 1024], F32)
            prm["bout"] = E.sbuf("bout_s", [128, 8], F32)
        else:
            for nm in ("qg", "qb", "kg", "kb"):
                prm[nm] = E.sbuf(nm + "_s", [128, 1], F32)

        @E.blk.gpsimd
        def _(g):
            n = 0
            for nm, ap in cdefs.items():
                g.dma_start(out=cstbuf[nm][:, :], in_=ap[:, :]).then_inc(d0.h, 16)
                n += 16
            if l1:
                g.dma_start(out=prm["cos2"][:, :], in_=cos2_d[:, :]).then_inc(d0.h, 16)
                g.dma_start(out=prm["sin2"][:, :], in_=sin2_d[:, :]).then_inc(d0.h, 16)
                for m in range(8):
                    g.dma_start(out=prm["bout"][:, m:m + 1],
                                in_=bout[128 * m:128 * (m + 1)].rearrange("(p o) -> p o", o=1)
                                ).then_inc(d0.h, 16)
                n += 16 * 10
            else:
                for nm, ap in (("qg", qg_d), ("qb", qb_d), ("kg", kg_d), ("kb", kb_d)):
                    for half in range(2):
                        g.dma_start(out=prm[nm][64 * half:64 * (half + 1), 0:1],
                                    in_=ap[:].rearrange("(p o) -> p o", o=1)
                                    ).then_inc(d0.h, 16)
                    n += 32
            g.wait_ge(d0.h, n)
            d0.n_fin = n
        d0.n += d0.n_fin

        if not l1:
            vp = E.S("ve_prep")
            @E.blk.vector
            def _(v):
                v.wait_ge(d0.h, d0.n)
                v.tensor_scalar(prm["qg"][:, :], prm["qg"][:, :], 0.125, None,
                                op0=ALU.mult).then_inc(vp.h, 1)
                v.wait_ge(vp.h, vp.n + 1)
                v.tensor_scalar(prm["qb"][:, :], prm["qb"][:, :], 0.125, None,
                                op0=ALU.mult).then_inc(vp.h, 1)
            vp.n += 2

        for nm in cdefs:
            cst[nm.replace("c_", "")] = cstbuf[nm]
        cst["idf"] = cstbuf["c_idf"]
        cst["idb"] = cstbuf["c_idb"]

        with contextlib.ExitStack() as sc0:
            emit_convert_w(E, sc0, wq, wq_bf.ap(),
                           wq.shape[0], wq.shape[1], tag="wq")
            emit_convert_w(E, sc0, wo, wo_bf.ap(), D_MODEL, D_MODEL, tag="wo")
        emit_barrier(E)

        npieces = max(1, n_kv // 1024)
        hn = n_kv // npieces
        with contextlib.ExitStack() as sc1:
            xtmp = E.sbuf("xtmp", [128, NCH * hn], F32, sc1)
            for h in range(npieces):
                with contextlib.ExitStack() as sc1b:
                    emit_load_x(E, sc1b, x[hn * h:hn * (h + 1), :], xtmp, hn,
                                cstbuf["c_idf"])
                emit_barrier(E)
                with contextlib.ExitStack() as sc1c:
                    emit_ln(E, sc1c, xtmp, xn, hn, cstbuf["c_ones_row"],
                            cstbuf["c_ones_col"], xn_stride=n_kv, xn_off=hn * h)
                emit_barrier(E)

        if l1:
            with contextlib.ExitStack() as sc2:
                emit_fenc(E, sc2, four_d, wfp, bfp, fenc_dr.ap(), prm, n_kv, cst)
            emit_barrier(E)

        with contextlib.ExitStack() as sc3:
            emit_attention(E, sc3, xn, attres, n_q, n_kv, cst, kind,
                           wq_bf.ap(), wo_bf.ap(), prm)
        emit_barrier(E)
        with contextlib.ExitStack() as sc4:
            emit_store_attn(E, sc4, attres, x, y, n_q, cst)
    return nc


def emit_fenc(E, scope, four_d, wfp, bfp, fenc_dr, prm, n_kv, cst):
    fbuf = E.sbuf("fe_f", [32, n_kv], BF16, scope)
    wf = E.sbuf("fe_w", [64, 32], F32, scope)
    wfT = E.sbuf("fe_wT", [32, 64], BF16, scope)
    bcol = E.sbuf("fe_b", [64, 1], F32, scope)
    fe = E.sbuf("fe_o", [64, n_kv], F32, scope)
    Pf = E.psum("fe_P", [128, 512], F32, scope)
    d = E.S("d_fe")
    pe = E.S("pe_fe")
    ve = E.S("ve_fe")
    db, peb, veb = d.n, pe.n, ve.n

    @E.blk.gpsimd
    def _(g):
        g.dma_start(out=fbuf[:, :], in_=four_d[:, :]).then_inc(d.h, 16)
        g.dma_start(out=wf[:, :], in_=wfp[:, :]).then_inc(d.h, 16)
        g.dma_start(out=bcol[:, :], in_=bfp[:].rearrange("(p o) -> p o", o=1)
                    ).then_inc(d.h, 16)
    d.n += 48

    @E.blk.tensor
    def _(t_):
        t_.wait_ge(d.h, db + 48)
        t_.transpose(Pf[0:32, 0:64], wf[:, :], cst["idf"][0:64, 0:64]).then_inc(pe.h, 1)
    pe.n += 1

    @E.blk.vector
    def _(v):
        v.wait_ge(pe.h, peb + 1)
        v.tensor_copy(wfT[:, :], Pf[0:32, 0:64]).then_inc(ve.h, 1)
    ve.n += 1

    nj = n_kv // 512
    @E.blk.tensor
    def _(t_):
        t_.wait_ge(ve.h, veb + 1)
        for j in range(nj):
            if j >= 1:
                t_.wait_ge(ve.h, veb + 1 + j)
            t_.matmul(Pf[0:64, 0:512], wfT[:, :],
                      fbuf[:, 512 * j:512 * (j + 1)],
                      start=True, stop=True).then_inc(pe.h, 1)
    pe.n += nj

    @E.blk.vector
    def _(v):
        for j in range(nj):
            v.wait_ge(pe.h, peb + 1 + j + 1)
            v.tensor_scalar(fe[:, 512 * j:512 * (j + 1)], Pf[0:64, 0:512],
                            bcol[:, 0:1], None, op0=ALU.add).then_inc(ve.h, 1)
    ve.n += nj

    d2 = E.S("d_fe2")
    d2b = d2.n
    @E.blk.gpsimd
    def _(q):
        q.wait_ge(ve.h, veb + 1 + nj)
        q.dma_start(out=fenc_dr[:, :], in_=fe[:, :]).then_inc(d2.h, 16)
        q.wait_ge(d2.h, d2b + 16)
        q.dma_start(out=prm["fenc2"][0:64, :], in_=fenc_dr[:, :]).then_inc(d2.h, 16)
        q.dma_start(out=prm["fenc2"][64:128, :], in_=fenc_dr[:, :]).then_inc(d2.h, 16)
    d2.n += 48


def emit_moe_ffn(E, scope, x_res, xn, n_tok, cst, gate_bf, gateb_bc, b1cols,
                 b2cols, w1_bf, w2_bf, res_scale=0.5, tag="m"):
    E8 = 8
    TT = 512
    ngrp = n_tok // TT
    NS = 8
    NP2 = 8

    w1r = E.sbuf(tag + "_w1r", [128, 2 * NCH * 512], BF16, scope)
    w2r = E.sbuf(tag + "_w2r", [128, 2 * 4 * D_MODEL], BF16, scope)
    hp = E.sbuf(tag + "_hp", [128, NF * TT], BF16, scope)
    acc = E.sbuf(tag + "_acc", [128, NCH * TT], F32, scope)
    tmp = E.sbuf(tag + "_tmp", [128, TT], F32, scope)
    selb = E.sbuf(tag + "_selb", [128, 2 * TT], F32, scope)
    selT = E.sbuf(tag + "_selT", [16, TT], F32, scope)
    logi = E.sbuf(tag + "_logi", [128, (TT // 128) * E8], F32, scope)
    seltm = E.sbuf(tag + "_seltm", [128, (TT // 128) * E8], F32, scope)
    sel2m = E.sbuf(tag + "_sel2m", [128, (TT // 128) * E8], F32, scope)
    srow = E.sbuf(tag + "_srow", [128, 10 * (TT // 128)], F32, scope)
    P = E.psum(tag + "_P", [128, 4096], F32, scope)

    d1 = [E.S("d_w1a_" + tag), E.S("d_w1b_" + tag)]
    d2 = [E.S("d_w2a_" + tag), E.S("d_w2b_" + tag)]
    pe = E.S("pe_" + tag)
    ac_ = E.S("ac_" + tag)
    ve = E.S("ve_" + tag)

    prev_step_ve = ve.n
    ntt = TT // 128
    PE_E = 72
    VE_E = 17

    for g in range(ngrp):
        pe0, ve0, ac0 = pe.n, ve.n, ac_.n
        psve = prev_step_ve

        @E.blk.tensor
        def _(t_, g=g, pe0=pe0, ve0=ve0, psve=psve):
            t_.wait_ge(ve.h, psve)
            for tt in range(ntt):
                gt = g * ntt + tt
                bank = 6 + (tt % 2)
                col = 512 * bank + 16 * (tt // 2)
                if tt >= 2:
                    t_.wait_ge(ve.h, ve0 + 14 * (tt - 2) + 1)
                for c in range(NCH):
                    mm = t_.matmul(
                        P[:, col:col + E8],
                        xn[:, n_tok * c + 128 * gt:n_tok * c + 128 * (gt + 1)],
                        gate_bf[:, c * 8:c * 8 + E8],
                        start=(c == 0), stop=(c == NCH - 1))
                    if c == NCH - 1:
                        mm.then_inc(pe.h, 1)
        pe.n += ntt

        @E.blk.vector
        def _(v, g=g, pe0=pe0, ve0=ve0, ac0=ac0):
            k = ve0
            for tt in range(ntt):
                bank = 6 + (tt % 2)
                col = 512 * bank + 16 * (tt // 2)
                L = logi[:, tt * E8:(tt + 1) * E8]
                S1 = seltm[:, tt * E8:(tt + 1) * E8]
                S2 = sel2m[:, tt * E8:(tt + 1) * E8]
                c0 = 10 * tt
                v.wait_ge(pe.h, pe0 + tt + 1)
                v.tensor_tensor(L, P[:, col:col + E8], gateb_bc[:, 0:E8],
                                op=ALU.add).then_inc(ve.h, 1); k += 1
                v.wait_ge(ve.h, k)
                v.tensor_reduce(srow[:, c0:c0 + 1], L, axis=AX.X, op=ALU.max,
                                negate=True).then_inc(ve.h, 1); k += 1
                v.wait_ge(ac_.h, ac0 + tt + 1)
                v.reciprocal(srow[:, c0 + 2:c0 + 3],
                             srow[:, c0 + 1:c0 + 2]).then_inc(ve.h, 1); k += 1
                v.wait_ge(ve.h, k)
                v.tensor_scalar(L, L, srow[:, c0 + 2:c0 + 3], None,
                                op0=ALU.mult).then_inc(ve.h, 1); k += 1
                v.wait_ge(ve.h, k)
                v.tensor_reduce(srow[:, c0 + 3:c0 + 4], L, axis=AX.X,
                                op=ALU.max).then_inc(ve.h, 1); k += 1
                v.wait_ge(ve.h, k)
                v.tensor_scalar(S1, L, srow[:, c0 + 3:c0 + 4], None,
                                op0=ALU.is_equal).then_inc(ve.h, 1); k += 1
                v.wait_ge(ve.h, k)
                v.tensor_tensor(S1, S1, cst["rank"][:, 0:E8],
                                op=ALU.mult).then_inc(ve.h, 1); k += 1
                v.wait_ge(ve.h, k)
                v.tensor_reduce(srow[:, c0 + 4:c0 + 5], S1, axis=AX.X,
                                op=ALU.max).then_inc(ve.h, 1); k += 1
                v.wait_ge(ve.h, k)
                v.tensor_scalar(S1, S1, srow[:, c0 + 4:c0 + 5], None,
                                op0=ALU.is_equal).then_inc(ve.h, 1); k += 1
                v.wait_ge(ve.h, k)
                v.scalar_tensor_tensor(S2, S1, -1e30, L, op0=ALU.mult,
                                       op1=ALU.add).then_inc(ve.h, 1); k += 1
                v.wait_ge(ve.h, k)
                v.tensor_reduce(srow[:, c0 + 5:c0 + 6], S2, axis=AX.X,
                                op=ALU.max).then_inc(ve.h, 1); k += 1
                v.wait_ge(ve.h, k)
                v.tensor_scalar(S2, S2, srow[:, c0 + 5:c0 + 6], None,
                                op0=ALU.is_equal).then_inc(ve.h, 1); k += 1
                v.wait_ge(ve.h, k)
                v.tensor_scalar(S1, S1, srow[:, c0 + 3:c0 + 4], None,
                                op0=ALU.mult).then_inc(ve.h, 1); k += 1
                v.wait_ge(ve.h, k)
                v.scalar_tensor_tensor(S1, S2, srow[:, c0 + 5:c0 + 6], S1,
                                       op0=ALU.mult, op1=ALU.add).then_inc(ve.h, 1)
                k += 1
        ve.n += 14 * ntt

        @E.blk.scalar
        def _(s, ve0=ve0, ac0=ac0):
            for tt in range(ntt):
                c0 = 10 * tt
                L = logi[:, tt * E8:(tt + 1) * E8]
                s.wait_ge(ve.h, ve0 + 14 * tt + 2)
                s.activation(out=L, in_=L, func=AF.Exp,
                             bias=srow[:, c0:c0 + 1],
                             accum_out=srow[:, c0 + 1:c0 + 2]).then_inc(ac_.h, 1)
        ac_.n += ntt

        pe1, ve1 = pe.n, ve.n
        @E.blk.tensor
        def _(t_, ve1=ve1):
            t_.wait_ge(ve.h, ve1)
            for tt in range(ntt):
                t_.transpose(P[0:E8, 3584 + 128 * tt:3584 + 128 * (tt + 1)],
                             seltm[:, tt * E8:(tt + 1) * E8],
                             cst["idf"][:, :]).then_inc(pe.h, 1)
        pe.n += ntt

        @E.blk.vector
        def _(v, pe1=pe1):
            v.wait_ge(pe.h, pe1 + ntt)
            v.tensor_copy(selT[0:E8, :], P[0:E8, 3584:3584 + TT]).then_inc(ve.h, 1)
        ve.n += 1
        sel_done = ve.n

        d1b = [s.n for s in d1]
        d2b = [s.n for s in d2]
        peb, acb, veb = pe.n, ac_.n, ve.n

        @E.blk.sync
        def _(q, d1b=d1b, peb=peb):
            for e in range(E8):
                for s in range(NS):
                    i = e * NS + s
                    if i >= 2:
                        i2 = i - 2
                        q.wait_ge(pe.h, peb + (i2 // NS) * PE_E + 4 * (i2 % NS) + 4)
                    else:
                        q.wait_ge(pe.h, peb)
                    src = w1_bf[e].rearrange("(c p) x -> p c x", p=128)
                    for half in range(2):
                        q.dma_start(
                            out=w1r[:, (i % 2) * NCH * 512 + half * 4 * 512:
                                    (i % 2) * NCH * 512 + (half + 1) * 4 * 512],
                            in_=src[:, 4 * half:4 * half + 4, 512 * s:512 * (s + 1)],
                        ).then_inc(d1[i % 2].h, 16)
        for i in range(E8 * NS):
            d1[i % 2].n += 32

        @E.blk.gpsimd
        def _(q, d2b=d2b, peb=peb):
            for e in range(E8):
                for j in range(NP2):
                    i = e * NP2 + j
                    if i >= 2:
                        i2 = i - 2
                        e2, j2 = i2 // NP2, i2 % NP2
                        f2 = 4 * j2 + 3
                        tgt = (peb + e2 * PE_E + 34 + f2 if f2 < NF - 1
                               else peb + e2 * PE_E + PE_E)
                        q.wait_ge(pe.h, tgt)
                    else:
                        q.wait_ge(pe.h, peb)
                    src = w2_bf[e][512 * j:512 * (j + 1), :].rearrange(
                        "(f p) x -> p f x", p=128)
                    q.dma_start(out=w2r[:, (i % 2) * 4 * D_MODEL:
                                        ((i % 2) + 1) * 4 * D_MODEL],
                                in_=src).then_inc(d2[i % 2].h, 16)
        for i in range(E8 * NP2):
            d2[i % 2].n += 16

        @E.blk.tensor
        def _(t_, g=g, d1b=d1b, d2b=d2b, peb=peb, acb=acb, veb=veb,
              sel_done=sel_done):
            for e in range(E8):
                for f in range(NF):
                    s = f // 4
                    i = e * NS + s
                    t_.wait_ge(d1[i % 2].h, d1b[i % 2] + 32 * (i // 2 + 1))
                    if e > 0 and f < 2:
                        t_.wait_ge(ve.h, veb + (e - 1) * VE_E + 2 * f + 3)
                    if f >= 2:
                        t_.wait_ge(ac_.h, acb + e * NF + f - 1)
                    off = (i % 2) * NCH * 512 + (f % 4) * 128
                    for c in range(NCH):
                        mm = t_.matmul(
                            P[:, (f % 2) * 512:(f % 2 + 1) * 512],
                            w1r[:, off + 512 * c:off + 512 * c + 128],
                            xn[:, n_tok * c + g * TT:n_tok * c + (g + 1) * TT],
                            start=(c == 0), stop=(c == NCH - 1))
                        if c == NCH - 1:
                            mm.then_inc(pe.h, 1)
                t_.wait_ge(ve.h, sel_done)
                if e > 0:
                    t_.wait_ge(ve.h, veb + (e - 1) * VE_E + VE_E)
                t_.matmul(P[:, 7 * 512:8 * 512],
                          cst["oh"][0:E8, 128 * e:128 * (e + 1)],
                          selT[0:E8, :], start=True, stop=True).then_inc(pe.h, 1)
                for f in range(NF):
                    j = f // 4
                    i = e * NP2 + j
                    t_.wait_ge(d2[i % 2].h, d2b[i % 2] + 16 * (i // 2 + 1))
                    if f == 0:
                        t_.wait_ge(ac_.h, acb + (e + 1) * NF)
                    if e > 0 and f == 0:
                        for m in range(2, 7):
                            t_.wait_ge(ve.h, veb + (e - 1) * VE_E + 2 * m + 3)
                    w2off = (i % 2) * 4 * D_MODEL + (f % 4) * D_MODEL
                    for m in range(8):
                        if m == 7 and f == 0:
                            t_.wait_ge(ve.h, veb + e * VE_E + 1)
                        mm = t_.matmul(
                            P[:, m * 512:(m + 1) * 512],
                            w2r[:, w2off + 128 * m:w2off + 128 * (m + 1)],
                            hp[:, f * TT:(f + 1) * TT],
                            start=(f == 0), stop=(f == NF - 1),
                            skip_group_check=True)
                        if (f < NF - 1 and m == 7) or f == NF - 1:
                            mm.then_inc(pe.h, 1)
        pe.n += E8 * PE_E

        @E.blk.scalar
        def _(s, peb=peb, acb=acb):
            for e in range(E8):
                for f in range(NF):
                    s.wait_ge(pe.h, peb + e * PE_E + f + 1)
                    s.wait_ge(ac_.h, acb + ((e - 1) * NF + f + 1 if e > 0 else 0))
                    s.activation(out=hp[:, f * TT:(f + 1) * TT],
                                 in_=P[:, (f % 2) * 512:(f % 2 + 1) * 512],
                                 func=AF.Relu,
                                 bias=b1cols[:, 32 * e + f:32 * e + f + 1]
                                 ).then_inc(ac_.h, 1)
        ac_.n += E8 * NF

        @E.blk.vector
        def _(v, peb=peb, veb=veb):
            k = veb
            for e in range(E8):
                v.wait_ge(pe.h, peb + e * PE_E + 33)
                v.wait_ge(ve.h, k)
                v.tensor_copy(selb[:, (e % 2) * TT:(e % 2 + 1) * TT],
                              P[:, 7 * 512:8 * 512]).then_inc(ve.h, 1); k += 1
                sb = selb[:, (e % 2) * TT:(e % 2 + 1) * TT]
                for m in range(8):
                    v.wait_ge(pe.h, peb + e * PE_E + 65 + m)
                    v.wait_ge(ve.h, k)
                    v.scalar_tensor_tensor(tmp[:, :], P[:, m * 512:(m + 1) * 512],
                                           b2cols[:, 8 * e + m:8 * e + m + 1], sb,
                                           op0=ALU.add, op1=ALU.mult).then_inc(ve.h, 1)
                    k += 1
                    v.wait_ge(ve.h, k)
                    if e == 0:
                        v.tensor_copy(acc[:, m * TT:(m + 1) * TT],
                                      tmp[:, :]).then_inc(ve.h, 1)
                    else:
                        v.tensor_tensor(acc[:, m * TT:(m + 1) * TT],
                                        acc[:, m * TT:(m + 1) * TT], tmp[:, :],
                                        op=ALU.add).then_inc(ve.h, 1)
                    k += 1
        ve.n += E8 * VE_E

        @E.blk.vector
        def _(v, g=g):
            k = ve.n
            for c in range(NCH):
                dst = slice(n_tok * c + g * TT, n_tok * c + (g + 1) * TT)
                v.wait_ge(ve.h, k)
                v.scalar_tensor_tensor(x_res[:, dst], acc[:, c * TT:(c + 1) * TT],
                                       res_scale, x_res[:, dst],
                                       op0=ALU.mult, op1=ALU.add).then_inc(ve.h, 1)
                k += 1
        ve.n += NCH
        prev_step_ve = ve.n


def build_ffn_l1(n_tok=T_CORE):
    nc = bass.Bass("TRN2", target_bir_lowering=False)
    x = nc.dram_tensor("x", [n_tok, D_MODEL], F32, kind="ExternalInput").ap()
    gw = nc.dram_tensor("gate_w", [D_MODEL, 8], F32, kind="ExternalInput").ap()
    gb = nc.dram_tensor("gate_b", [8], F32, kind="ExternalInput").ap()
    w1 = nc.dram_tensor("w1", [8, D_MODEL, FF], F32, kind="ExternalInput").ap()
    b1 = nc.dram_tensor("b1", [8, FF], F32, kind="ExternalInput").ap()
    w2 = nc.dram_tensor("w2", [8, FF, D_MODEL], F32, kind="ExternalInput").ap()
    b2 = nc.dram_tensor("b2", [8, D_MODEL], F32, kind="ExternalInput").ap()
    cdefs = {}
    for nm in ("c_idf", "c_ones_row", "c_ones_col", "c_oh", "c_rank8"):
        shp = {"c_idf": [128, 128], "c_ones_row": [1, 128], "c_ones_col": [128, 1],
               "c_oh": [16, 16 * 128], "c_rank8": [128, 8]}[nm]
        cdefs[nm] = nc.dram_tensor(nm, shp, F32, kind="ExternalInput").ap()
    y = nc.dram_tensor("y", [n_tok, D_MODEL], F32, kind="ExternalOutput").ap()

    w1_bf = nc.dram_tensor("w1_bf", [8, D_MODEL, FF], BF16)
    w2_bf = nc.dram_tensor("w2_bf", [8, FF, D_MODEL], BF16)

    with contextlib.ExitStack() as st:
        blk = st.enter_context(nc.Block())
        E = Ctx(nc, blk, st)
        x_res = E.sbuf("x_res", [128, NCH * n_tok], F32)
        xn = E.sbuf("xn", [128, NCH * n_tok], BF16)
        cstbuf = {nm: E.sbuf(nm + "_s", list(ap.shape), F32)
                  for nm, ap in cdefs.items()}
        gate_bf = E.sbuf("gate_bf", [128, 64], BF16)
        gbrow = E.sbuf("gbrow", [1, 8], F32)
        gateb_bc = E.sbuf("gateb_bc", [128, 8], F32)
        b1cols = E.sbuf("b1cols", [128, 256], F32)
        b2cols = E.sbuf("b2cols", [128, 64], F32)
        sc0 = contextlib.ExitStack()
        btmp = E.sbuf("btmp", [32, 128], F32, sc0)
        Pb = E.psum("Pb", [128, 512], F32, sc0)

        d0 = E.S("d_init")
        pe0 = E.S("pe_init")
        ve0 = E.S("ve_init")

        @E.blk.gpsimd
        def _(g):
            n = 0
            for nm, ap in cdefs.items():
                g.dma_start(out=cstbuf[nm][:, :], in_=ap[:, :]).then_inc(d0.h, 16)
                n += 16
            for c in range(NCH):
                g.dma_start(out=gate_bf[:, 8 * c:8 * (c + 1)],
                            in_=gw[128 * c:128 * (c + 1), :]).then_inc(d0.h, 16)
                n += 16
            g.dma_start(out=gbrow[:, :],
                        in_=gb[:].rearrange("(o e) -> o e", o=1)).then_inc(d0.h, 16)
            n += 16
            d0.n_fin = n
        d0.n += d0.n_fin

        @E.blk.tensor
        def _(t_):
            t_.wait_ge(d0.h, d0.n)
            t_.matmul(Pb[:, 0:8], cstbuf["c_ones_row"][0:1, :], gbrow[0:1, :],
                      start=True, stop=True).then_inc(pe0.h, 1)
        pe0.n += 1

        @E.blk.vector
        def _(v):
            v.wait_ge(pe0.h, pe0.n)
            v.tensor_copy(gateb_bc[:, :], Pb[:, 0:8]).then_inc(ve0.h, 1)
        ve0.n += 1

        db1 = E.S("d_b12")
        for e in range(8):
            dbb = db1.n
            @E.blk.gpsimd
            def _(g, e=e, dbb=dbb):
                g.wait_ge(ve0.h, ve0.n)
                g.dma_start(out=btmp[:, :],
                            in_=b1[e].rearrange("(f x) -> f x", f=32)).then_inc(db1.h, 16)
            db1.n += 16

            @E.blk.tensor
            def _(t_, e=e, dbb=dbb):
                t_.wait_ge(db1.h, dbb + 16)
                t_.wait_ge(ve0.h, ve0.n)
                t_.transpose(Pb[:, 0:32], btmp[:, :],
                             cstbuf["c_idf"][0:32, 0:32]).then_inc(pe0.h, 1)
            pe0.n += 1

            @E.blk.vector
            def _(v, e=e):
                v.wait_ge(pe0.h, pe0.n)
                v.tensor_copy(b1cols[:, 32 * e:32 * (e + 1)],
                              Pb[:, 0:32]).then_inc(ve0.h, 1)
            ve0.n += 1

            dbb2 = db1.n
            @E.blk.gpsimd
            def _(g, e=e, dbb2=dbb2):
                g.wait_ge(ve0.h, ve0.n)
                g.dma_start(out=btmp[0:8, :],
                            in_=b2[e].rearrange("(f x) -> f x", f=8)).then_inc(db1.h, 16)
            db1.n += 16

            @E.blk.tensor
            def _(t_, e=e, dbb2=dbb2):
                t_.wait_ge(db1.h, dbb2 + 16)
                t_.wait_ge(ve0.h, ve0.n)
                t_.transpose(Pb[:, 0:8], btmp[0:8, :],
                             cstbuf["c_idf"][0:8, 0:8]).then_inc(pe0.h, 1)
            pe0.n += 1

            @E.blk.vector
            def _(v, e=e):
                v.wait_ge(pe0.h, pe0.n)
                v.tensor_copy(b2cols[:, 8 * e:8 * (e + 1)],
                              Pb[:, 0:8]).then_inc(ve0.h, 1)
            ve0.n += 1

        emit_load_x(E, sc0, x, x_res, n_tok, cstbuf["c_idf"])
        emit_convert_w(E, sc0, w1.rearrange("e a b -> (e a) b"),
                       w1_bf.ap().rearrange("e a b -> (e a) b"),
                       8 * D_MODEL, FF, tag="w1")
        emit_convert_w(E, sc0, w2.rearrange("e a b -> (e a) b"),
                       w2_bf.ap().rearrange("e a b -> (e a) b"),
                       8 * FF, D_MODEL, tag="w2")
        sc0.close()
        emit_barrier(E)
        with contextlib.ExitStack() as sc1:
            emit_ln(E, sc1, x_res, xn, n_tok, cstbuf["c_ones_row"],
                    cstbuf["c_ones_col"])
        emit_barrier(E)
        with contextlib.ExitStack() as sc2:
            cst = {"idf": cstbuf["c_idf"], "oh": cstbuf["c_oh"],
                   "rank": cstbuf["c_rank8"]}
            emit_moe_ffn(E, sc2, x_res, xn, n_tok, cst, gate_bf, gateb_bc,
                         b1cols, b2cols, w1_bf.ap(), w2_bf.ap())
        emit_barrier(E)
        with contextlib.ExitStack() as sc3:
            emit_store_x(E, sc3, x_res, y, n_tok, cstbuf["c_idf"])
    return nc


def _rot(xf, core):
    b, p = core // 2, core % 2
    own = xf[b * 4096 + p * 2048: b * 4096 + p * 2048 + 2048]
    oth = xf[b * 4096 + (1 - p) * 2048: b * 4096 + (1 - p) * 2048 + 2048]
    return np.ascontiguousarray(np.concatenate([own, oth], 0))


def kernel(**inputs):
    cst = host_consts()
    xf = np.asarray(inputs["L0_attn_x"], np.float32).reshape(-1, D_MODEL)
    cores = list(range(N_CORES))

    def shards(x):
        return [np.ascontiguousarray(x[c * T_CORE:(c + 1) * T_CORE]) for c in cores]

    nc_f0 = build_ffn_l0()
    f0_w = dict(routers=np.asarray(inputs["L0_ffn_routers"], np.float32),
                w1=np.asarray(inputs["L0_ffn_W1"], np.float32),
                w2=np.asarray(inputs["L0_ffn_W2"], np.float32),
                **cst)

    def run_f0(x):
        maps = [dict(x=s, **f0_w) for s in shards(x)]
        res = run_bass_kernel_spmd(nc_f0, maps, core_ids=cores)
        return np.concatenate([r["y"] for r in res.results], 0)

    x = run_f0(xf)

    nc_a0 = build_attn("l0", T_CORE, 2 * T_CORE)
    a0_w = dict(w_qkv=np.asarray(inputs["L0_attn_w_qkv"], np.float32),
                w_proj=np.asarray(inputs["L0_attn_w_proj"], np.float32),
                q_gamma=np.asarray(inputs["L0_attn_q_gamma"], np.float32),
                q_beta=np.asarray(inputs["L0_attn_q_beta"], np.float32),
                k_gamma=np.asarray(inputs["L0_attn_k_gamma"], np.float32),
                k_beta=np.asarray(inputs["L0_attn_k_beta"], np.float32),
                **{k: v for k, v in cst.items()
                   if k in ("c_idf", "c_idb", "c_ones_row", "c_ones_col",
                            "c_hmask_f", "c_hmask_b", "c_sel2", "c_prot2")})
    maps = [dict(x=_rot(x, c), **a0_w) for c in cores]
    res = run_bass_kernel_spmd(nc_a0, maps, core_ids=cores)
    x = np.concatenate([r["y"] for r in res.results], 0)

    x = run_f0(x)

    nc_f1 = build_ffn_l1()
    f1_w = dict(gate_w=np.asarray(inputs["L1_ffn_gate_w"], np.float32),
                gate_b=np.asarray(inputs["L1_ffn_gate_b"], np.float32),
                w1=np.asarray(inputs["L1_ffn_w1"], np.float32),
                b1=np.asarray(inputs["L1_ffn_b1"], np.float32),
                w2=np.asarray(inputs["L1_ffn_w2"], np.float32),
                b2=np.asarray(inputs["L1_ffn_b2"], np.float32),
                **{k: v for k, v in cst.items()
                   if k in ("c_idf", "c_ones_row", "c_ones_col", "c_oh",
                            "c_rank8")})

    def run_f1(x):
        maps = [dict(x=s, **f1_w) for s in shards(x)]
        res = run_bass_kernel_spmd(nc_f1, maps, core_ids=cores)
        return np.concatenate([r["y"] for r in res.results], 0)

    x = run_f1(x)

    nc_a1 = build_attn("l1", T_CORE, 2 * T_CORE)
    a1_w = dict(w_qkv=np.asarray(inputs["L1_attn_w_qkv"], np.float32),
                w_out=np.asarray(inputs["L1_attn_w_out"], np.float32),
                w_fproj=np.asarray(inputs["L1_attn_w_fproj"], np.float32),
                b_fproj=np.asarray(inputs["L1_attn_b_fproj"], np.float32),
                b_out=np.asarray(inputs["L1_attn_b_out"], np.float32),
                **{k: v for k, v in cst.items()
                   if k in ("c_idf", "c_idb", "c_ones_row", "c_ones_col",
                            "c_hmask_f", "c_hmask_b", "c_sel2", "c_prot2")})
    maps = []
    for c in cores:
        p = c % 2
        order = np.concatenate([p * 2048 + np.arange(2048),
                                (1 - p) * 2048 + np.arange(2048)])
        maps.append(dict(x=_rot(x, c), **a1_w, **rope_tables(2 * T_CORE, order)))
    res = run_bass_kernel_spmd(nc_a1, maps, core_ids=cores)
    x = np.concatenate([r["y"] for r in res.results], 0)

    x = run_f1(x)
    return x.reshape(4, 4096, D_MODEL)
